# version 21
# baseline (speedup 1.0000x reference)
import numpy as np
import ml_dtypes
from contextlib import ExitStack

import concourse.bass as bass
import concourse.mybir as mybir
from concourse.bass_utils import run_bass_kernel_spmd

F32 = mybir.dt.float32
BF16 = mybir.dt.bfloat16
AF = mybir.ActivationFunctionType
ALU = mybir.AluOpType
AX = mybir.AxisListType

SCALE = 1.0 / float(np.sqrt(128.0))
CSHIFT = 16.0
EPS = 1e-5


class Cfg:
    def __init__(self, B, S, D, DFF):
        self.B = B
        self.S = S
        self.D = D
        self.DFF = DFF
        self.H = D // 128
        self.ND = D // 128
        self.NS = S // 128
        self.NSB = S // 512
        self.NEB = D // 512
        self.NE1 = DFF // 128
        self.NE2 = DFF // 128
        self.STILE = 512
        self.NT = S // self.STILE


FULL = Cfg(B=2, S=2048, D=2048, DFF=4096)


class Sem:
    __slots__ = ("h", "n", "name")


class Bank:
    __slots__ = ("t", "free")

    def __init__(self, t):
        self.t = t
        self.free = None


class Ctx:
    ENGS = ("pe", "act", "dve", "sp", "gp")

    def __init__(self, nc, stack):
        self.nc = nc
        self.stack = stack
        self.ops = {e: [] for e in self.ENGS}
        self.waited = {e: {} for e in self.ENGS}
        self.all_sems = []
        self.no_barrier = set()
        self.sems = {}
        for name in ("pe", "act", "dve"):
            self.sems[name] = self.new_sem(name)

    def new_sem(self, name, no_barrier=False):
        s = Sem()
        s.h = self.stack.enter_context(self.nc.semaphore("sm_" + name))
        s.n = 0
        s.name = name
        self.all_sems.append(s)
        if no_barrier:
            self.no_barrier.add(name)
        return s

    def emit(self, eng, fn, waits=(), inc=None, inck=1):
        lst = self.ops[eng]
        wt = self.waited[eng]
        for w in waits:
            if w is None:
                continue
            s, v = w
            if v is None or v <= 0:
                continue
            if wt.get(s.name, 0) >= v:
                continue
            wt[s.name] = v
            lst.append(lambda e, s=s, v=v: e.wait_ge(s.h, v))
        if inc is not None:
            inc.n += inck
            tok = inc.n

            def run(e, fn=fn, inc=inc, k=inck):
                fn(e).then_inc(inc.h, k)

            lst.append(run)
            return tok
        lst.append(fn)
        return None

    def pe(self, fn, waits=(), inc=True):
        return self.emit("pe", fn, waits, self.sems["pe"] if inc else None)

    def act(self, fn, waits=(), inc=True):
        return self.emit("act", fn, waits, self.sems["act"] if inc else None)

    def dve(self, fn, waits=(), inc=True):
        return self.emit("dve", fn, waits, self.sems["dve"] if inc else None)

    def ld(self, sem, out, in_, waits=(), transpose=False):
        if transpose:
            fn = lambda e: e.dma_start(out, in_, transpose=True)
        else:
            fn = lambda e: e.dma_start(out, in_)
        self.emit("sp", fn, waits, sem, 16)
        return (sem, sem.n)

    def stq(self, sem, out, in_, waits=()):
        fn = lambda e: e.dma_start(out, in_)
        self.emit("act", fn, waits, sem, 16)
        return (sem, sem.n)

    def gpd(self, sem, out, in_, waits=()):
        fn = lambda e: e.dma_start(out, in_)
        self.emit("gp", fn, waits, sem, 16)
        return (sem, sem.n)

    def barrier(self):
        toks = [(s, s.n) for s in self.all_sems if s.name not in self.no_barrier]
        for e in self.ENGS:
            for s, v in toks:
                if v <= 0 or self.waited[e].get(s.name, 0) >= v:
                    continue
                self.waited[e][s.name] = v
                self.ops[e].append(lambda eng, s=s, v=v: eng.wait_ge(s.h, v))


def build_program(cfg, debug=False):
    nc = bass.Bass("TRN2", target_bir_lowering=False,
                   dynamic_dma_scratch_size=4096)
    DBG = "ExternalOutput" if debug else "Internal"
    B, S, D, DFF = cfg.B, cfg.S, cfg.D, cfg.DFF
    H, ND, NS, NSB, NEB = cfg.H, cfg.ND, cfg.NS, cfg.NSB, cfg.NEB
    NE1, NE2, STILE, NT = cfg.NE1, cfg.NE2, cfg.STILE, cfg.NT

    x_t = nc.dram_tensor("x", [B, S, D], F32, kind="ExternalInput")
    wqT_t = nc.dram_tensor("wqT", [H * 128, ND * 128], F32, kind="ExternalInput")
    wkT_t = nc.dram_tensor("wkT", [H * 128, ND * 128], F32, kind="ExternalInput")
    wvT_t = nc.dram_tensor("wvT", [NEB * 128, ND * 512], F32, kind="ExternalInput")
    woT_t = nc.dram_tensor("woT", [NEB * 128, H * 512], F32, kind="ExternalInput")
    w1T_t = nc.dram_tensor("w1T", [NE1 * 128, ND * 128], F32, kind="ExternalInput")
    w2T_t = nc.dram_tensor("w2T", [NE2 * 128, NE1 * 128], F32, kind="ExternalInput")
    w3T_t = nc.dram_tensor("w3T", [NEB * 128, NE2 * 512], F32, kind="ExternalInput")
    b1t_t = nc.dram_tensor("b1t", [128, NE1], F32, kind="ExternalInput")
    b2t_t = nc.dram_tensor("b2t", [128, NE2], F32, kind="ExternalInput")
    b3r_t = nc.dram_tensor("b3row", [1, D], F32, kind="ExternalInput")
    cos_t = nc.dram_tensor("cos_t", [128, S], F32, kind="ExternalInput")
    sin_t = nc.dram_tensor("sin_t", [128, S], F32, kind="ExternalInput")
    mask_t = nc.dram_tensor("masks", [128, 896], BF16, kind="ExternalInput")
    out_t = nc.dram_tensor("out", [B, S, D], F32, kind="ExternalOutput")

    wq_bf = nc.dram_tensor("wq_bf", [H * 128, ND * 128], BF16, kind="Internal")
    wk_bf = nc.dram_tensor("wk_bf", [H * 128, ND * 128], BF16, kind="Internal")
    wv_bf = nc.dram_tensor("wv_bf", [NEB * 128, ND * 512], BF16, kind="Internal")
    wo_bf = nc.dram_tensor("wo_bf", [NEB * 128, H * 512], BF16, kind="Internal")
    w1_bf = nc.dram_tensor("w1_bf", [NE1 * 128, ND * 128], BF16, kind="Internal")
    w2_bf = nc.dram_tensor("w2_bf", [NE2 * 128, NE1 * 128], BF16, kind="Internal")
    w3_bf = nc.dram_tensor("w3_bf", [NEB * 128, NE2 * 512], BF16, kind="Internal")
    xb_hbm = nc.dram_tensor("xb_hbm", [B, S, D], BF16, kind="Internal")
    qT_hbm = nc.dram_tensor("qT_hbm", [B, H, 128, S], BF16, kind=DBG)
    kT_hbm = nc.dram_tensor("kT_hbm", [B, H, 128, S], BF16, kind=DBG)
    v_hbm = nc.dram_tensor("v_hbm", [B, H, 128, NS * 128], BF16, kind=DBG)
    oT_hbm = nc.dram_tensor("oT_hbm", [B, NSB, 128, H * 512], BF16, kind=DBG)
    y_hbm = nc.dram_tensor("y_hbm", [B, S, D], F32, kind=DBG)
    yb_hbm = nc.dram_tensor("yb_hbm", [B, S, D], BF16, kind="Internal")
    dbgA_t = None
    if debug:
        NTIL = 2 * NSB * NSB + 2 * NSB
        dbgA_t = nc.dram_tensor("dbgA", [H, NTIL, 128, 512], BF16,
                                kind="ExternalOutput")

    xap = x_t.ap()
    outap = out_t.ap()

    st = ExitStack()
    with st:
        C = Ctx(nc, st)
        s_pe, s_act, s_dve = C.sems["pe"], C.sems["act"], C.sems["dve"]

        m_init = C.new_sem("ini")
        m_cw = {nm: C.new_sem("c" + nm, no_barrier=True)
                for nm in ("x0", "x1", "wq", "wk", "wv", "wo", "w1", "w2", "w3")}
        m_xt = C.new_sem("xt")
        m_qkp = [C.new_sem("qkp0"), C.new_sem("qkp1")]
        m_wv = [C.new_sem("wv0"), C.new_sem("wv1")]
        m_qkv = [C.new_sem("qkv0"), C.new_sem("qkv1")]
        m_otg = [C.new_sem("otg0"), C.new_sem("otg1")]
        m_wop = [C.new_sem("wop0"), C.new_sem("wop1")]
        m_xc = [C.new_sem("xc0"), C.new_sem("xc1")]
        m_yt = C.new_sem("yt")
        m_w1 = [C.new_sem("w10"), C.new_sem("w11")]
        m_w2 = [C.new_sem("w20"), C.new_sem("w21")]
        m_w3 = [C.new_sem("w30"), C.new_sem("w31")]
        m_yl = [C.new_sem("yl0"), C.new_sem("yl1")]
        m_sqk = [C.new_sem("sqk0"), C.new_sem("sqk1")]
        m_sv = [C.new_sem("sv0"), C.new_sem("sv1")]
        m_sot = [C.new_sem("sot0"), C.new_sem("sot1")]
        m_sy = [C.new_sem(f"sy{i}") for i in range(4)]
        m_sout = C.new_sem("sout")
        m_gy = [C.new_sem(f"gy{i}") for i in range(4)]
        m_dbg = C.new_sem("dbg", no_barrier=True)

        WA = max(ND * S, NE1 * STILE + NE2 * 512)
        NCH1 = (2 * ND * 128) // 512
        WB = max(2 * ND * 128 + 2 * ND * 512,
                 6 * S + 4 * 512,
                 4 * H * 512,
                 ND * STILE + 2 * ND * 128 + 2 * NE1 * 128 + (NE2 - NCH1) * 512)
        WF = max(2 * S + 4 * 512, 5 * D)
        AA = nc.alloc_sbuf_tensor("arena_a", [128, WA], BF16)
        AB = nc.alloc_sbuf_tensor("arena_b", [128, WB], BF16)
        FA = nc.alloc_sbuf_tensor("farena", [128, WF], F32)
        masks = nc.alloc_sbuf_tensor("masks_sb", [128, 896], BF16)
        stats = nc.alloc_sbuf_tensor("stats", [128, 8], F32)
        consts = nc.alloc_sbuf_tensor("consts", [128, 4], F32)
        b1t_sb = nc.alloc_sbuf_tensor("b1t_sb", [128, NE1], F32)
        b2t_sb = nc.alloc_sbuf_tensor("b2t_sb", [128, NE2], F32)
        b3f_sb = nc.alloc_sbuf_tensor("b3f_sb", [1, D], F32)
        b3b_sb = nc.alloc_sbuf_tensor("b3b_sb", [1, D], BF16)
        ones_cb = nc.alloc_sbuf_tensor("ones_cb", [128, 1], BF16)
        ones_rb = nc.alloc_sbuf_tensor("ones_rb", [1, 128], BF16)
        ones_rf = nc.alloc_sbuf_tensor("ones_rf", [1, 128], F32)
        linv_sb = nc.alloc_sbuf_tensor("linv_sb", [1, 512], F32)
        linv_hi = nc.alloc_sbuf_tensor("linv_hi", [1, 512], BF16)
        linv_hif = nc.alloc_sbuf_tensor("linv_hif", [1, 512], F32)
        linv_lo = nc.alloc_sbuf_tensor("linv_lo", [1, 512], BF16)
        lrep_sb = nc.alloc_sbuf_tensor("lrep_sb", [128, 2 * 512], F32)
        cos_sb = nc.alloc_sbuf_tensor("cos_sb", [128, S], F32)
        sin_sb = nc.alloc_sbuf_tensor("sin_sb", [128, S], F32)
        qk_stage = nc.alloc_sbuf_tensor("qk_stage", [128, 2 * 512], BF16)
        v_stage = nc.alloc_sbuf_tensor("v_stage", [128, 2 * 512], BF16)
        ot_stage = nc.alloc_sbuf_tensor("ot_stage", [128, 2 * 512], BF16)
        xc_sb = nc.alloc_sbuf_tensor("xc_sb", [128, 2 * 512], F32)

        banks = [Bank(st.enter_context(nc.psum_tensor(f"pb{i}", [128, 512], F32)))
                 for i in range(8)]

        NEGC, SCL, EPSC, NEG1 = 0, 1, 2, 3
        cvec = lambda i: consts[:, i:i + 1]

        C.dve(lambda e: e.memset(consts[:, NEGC:NEGC + 1], -CSHIFT), inc=False)
        C.dve(lambda e: e.memset(consts[:, SCL:SCL + 1], SCALE), inc=False)
        C.dve(lambda e: e.memset(consts[:, EPSC:EPSC + 1], EPS), inc=False)
        C.dve(lambda e: e.memset(consts[:, NEG1:NEG1 + 1], -1.0), inc=False)
        C.dve(lambda e: e.memset(ones_cb[:, :], 1.0), inc=False)
        C.dve(lambda e: e.memset(ones_rb[:, :], 1.0), inc=False)
        C.dve(lambda e: e.memset(ones_rf[:, :], 1.0))
        C.ld(m_init, masks[:, :], mask_t.ap()[:, :])
        C.ld(m_init, cos_sb[:, :], cos_t.ap()[:, :])
        C.ld(m_init, sin_sb[:, :], sin_t.ap()[:, :])
        C.ld(m_init, b1t_sb[:, :], b1t_t.ap()[:, :])
        C.ld(m_init, b2t_sb[:, :], b2t_t.ap()[:, :])
        tk_ini = C.ld(m_init, b3f_sb[:, :], b3r_t.ap()[:, :])
        tk_b3b = C.dve(lambda e: e.tensor_copy(b3b_sb[:, :], b3f_sb[:, :]),
                       waits=[tk_ini])

        def cast_dram(sem, dst_t, src_t, rows, chunk_rows):
            tok = None
            r = 0
            while r < rows:
                n = min(chunk_rows, rows - r)
                tok = C.gpd(sem, dst_t.ap()[r:r + n, :], src_t.ap()[r:r + n, :])
                r += n
            return tok

        def cast_x(b):
            for r in range(0, S, 512):
                C.gpd(m_cw[f"x{b}"], xb_hbm.ap()[b, r:r + 512, :],
                      xap[b, r:r + 512, :])
            return (m_cw[f"x{b}"], m_cw[f"x{b}"].n)

        tk_xcast = [None] * B
        tk_xcast[0] = cast_x(0)
        tk_wq = cast_dram(m_cw["wq"], wq_bf, wqT_t, H * 128, 1024)
        tk_wk = cast_dram(m_cw["wk"], wk_bf, wkT_t, H * 128, 1024)
        tk_wv = cast_dram(m_cw["wv"], wv_bf, wvT_t, NEB * 128, 256)
        if B > 1:
            tk_xcast[1] = cast_x(1)
        tk_wo = cast_dram(m_cw["wo"], wo_bf, woT_t, NEB * 128, 256)
        tk_w1 = cast_dram(m_cw["w1"], w1_bf, w1T_t, NE1 * 128, 1024)
        tk_w2 = cast_dram(m_cw["w2"], w2_bf, w2T_t, NE2 * 128, 512)
        tk_w3 = cast_dram(m_cw["w3"], w3_bf, w3T_t, NEB * 128, 128)

        xT_o = 0
        h1_o = 0
        w3slotA = NE1 * STILE
        qkc_o = 0
        wvs_o = 2 * ND * 128
        qb_o = 0
        kb_o = 2 * S
        vb_o = 4 * S
        ab_o = 6 * S
        otg_o = 0
        wop_o = 2 * H * 512
        yT_o = 0
        w1c_o = ND * STILE
        w2c_o = ND * STILE + 2 * ND * 128
        h2_base2 = ND * STILE + 2 * ND * 128 + 2 * NE1 * 128

        def h2_off(j2):
            if j2 < NCH1:
                return w1c_o + j2 * 512
            return h2_base2 + (j2 - NCH1) * 512

        tmp_o = 0
        u_o = 2 * 512
        r1_o = 0
        sq_o = 4 * D

        def layernorm(r_ap, prev, rdy=()):
            w = [(s_dve, prev.get("apply")), (s_act, prev.get("sqrt"))] + list(rdy)
            t_sum = C.dve(lambda e: e.reduce_sum(stats[:, 0:1], r_ap, axis=AX.X),
                          waits=w)
            t_sq = C.act(lambda e: e.activation(FA[:, sq_o:sq_o + D], r_ap, AF.Square,
                                                accum_out=stats[:, 1:2]),
                         waits=[(s_dve, prev.get("var"))] + list(rdy))
            t_nm = C.dve(lambda e: e.tensor_scalar(stats[:, 2:3], stats[:, 0:1],
                                                   -1.0 / D, None, op0=ALU.mult),
                         waits=[(s_dve, t_sum)])
            t_v1 = C.dve(lambda e: e.tensor_scalar(stats[:, 3:4], stats[:, 1:2],
                                                   1.0 / D, None, op0=ALU.mult),
                         waits=[(s_act, t_sq)])
            t_m2 = C.dve(lambda e: e.tensor_mul(stats[:, 6:7], stats[:, 2:3],
                                                stats[:, 2:3]),
                         waits=[(s_dve, t_nm)])
            t_var = C.dve(lambda e: e.tensor_sub(stats[:, 3:4], stats[:, 3:4],
                                                 stats[:, 6:7]),
                          waits=[(s_dve, t_m2)])
            t_sd = C.act(lambda e: e.activation(stats[:, 4:5], stats[:, 3:4], AF.Sqrt,
                                                bias=cvec(EPSC)),
                         waits=[(s_dve, t_var)])
            t_rs = C.dve(lambda e: e.reciprocal(stats[:, 5:6], stats[:, 4:5]),
                         waits=[(s_act, t_sd)])
            t_ap = C.dve(lambda e: e.tensor_scalar(r_ap, r_ap, stats[:, 2:3],
                                                   stats[:, 5:6], op0=ALU.add,
                                                   op1=ALU.mult),
                         waits=[(s_dve, t_rs)])
            prev["apply"] = t_ap
            prev["sqrt"] = t_sd
            prev["var"] = t_var
            return t_ap

        prev_f_last = None
        prev_ln_sqrt = None
        for b in range(B):
            if b == 0:
                C.barrier()
            for dc in range(ND):
                tk_xT = C.ld(m_xt, AA[:, xT_o + dc * S: xT_o + (dc + 1) * S],
                             xb_hbm.ap()[b, :, dc * 128:(dc + 1) * 128],
                             waits=[tk_xcast[b], (s_pe, prev_f_last)], transpose=True)

            if b == 0:
                C.barrier()
            pan_tok = [None, None]
            pan_freed = [None, None]
            stage_store = [(m, m.n) for m in m_sqk]
            tmp_done = [None, None]
            it = 0
            for (wsrc, wtok, dst) in ((wq_bf, tk_wq, qT_hbm), (wk_bf, tk_wk, kT_hbm)):
                for h in range(H):
                    bi = h % 2
                    off = qkc_o + bi * ND * 128
                    pan_tok[bi] = C.ld(
                        m_qkp[bi],
                        AB[:, off:off + ND * 128],
                        wsrc.ap()[h * 128:(h + 1) * 128, :],
                        waits=[wtok, pan_freed[bi], (s_pe, prev_f_last)])
                    for sb in range(NSB):
                        bk = banks[it % 2]
                        for dc in range(ND):
                            wl = []
                            if dc == 0:
                                wl = [pan_tok[bi], tk_xT, bk.free]
                            tmm = C.pe(lambda e, bk=bk, off=off, dc=dc, sb=sb: e.matmul(
                                bk.t[:, :],
                                AB[:, off + dc * 128: off + (dc + 1) * 128],
                                AA[:, xT_o + dc * S + sb * 512: xT_o + dc * S + (sb + 1) * 512],
                                start=(dc == 0), stop=(dc == ND - 1)),
                                waits=wl, inc=(dc == ND - 1))
                        pan_freed[bi] = (s_pe, tmm)
                        r = it % 2
                        to = tmp_o + r * 512
                        uo = u_o + r * 512
                        so = r * 512
                        cs = cos_sb[:, sb * 512:(sb + 1) * 512]
                        sn = sin_sb[:, sb * 512:(sb + 1) * 512]
                        C.dve(lambda e, bk=bk, to=to, sn=sn: e.tensor_tensor(
                            FA[0:64, to:to + 512], bk.t[64:128, :], sn[0:64, :], op=ALU.mult),
                            waits=[(s_pe, tmm), (s_dve, tmp_done[r]), tk_ini,
                                   (s_act, prev_ln_sqrt), (m_sout, m_sout.n)])
                        C.dve(lambda e, bk=bk, to=to, sn=sn: e.tensor_tensor(
                            FA[64:128, to:to + 512], bk.t[0:64, :], sn[64:128, :], op=ALU.mult))
                        tc = C.dve(lambda e, bk=bk, uo=uo, cs=cs: e.tensor_tensor(
                            FA[:, uo:uo + 512], bk.t[:, :], cs[:, :], op=ALU.mult))
                        bk.free = (s_dve, tc)
                        td = C.dve(lambda e, to=to, uo=uo, so=so: e.tensor_add(
                            qk_stage[:, so:so + 512], FA[:, uo:uo + 512], FA[:, to:to + 512]),
                            waits=[(s_dve, tc), stage_store[r]])
                        tmp_done[r] = td
                        stage_store[r] = C.stq(m_sqk[r],
                                               dst.ap()[b, h, :, sb * 512:(sb + 1) * 512],
                                               qk_stage[:, so:so + 512],
                                               waits=[(s_dve, td)])
                        it += 1
            vp_tok = [None, None]
            vp_freed = [None, None]
            vs_store = [(m, m.n) for m in m_sv]
            it = 0
            for eb in range(NEB):
                bi = eb % 2
                off = wvs_o + bi * ND * 512
                vp_tok[bi] = C.ld(
                    m_wv[bi],
                    AB[:, off:off + ND * 512],
                    wv_bf.ap()[eb * 128:(eb + 1) * 128, :],
                    waits=[tk_wv, vp_freed[bi], (s_pe, prev_f_last)])
                for sc in range(NS):
                    bk = banks[2 + (it % 2)]
                    for dc in range(ND):
                        wl = []
                        if dc == 0:
                            wl = [vp_tok[bi], bk.free]
                        tmm = C.pe(lambda e, bk=bk, off=off, dc=dc, sc=sc: e.matmul(
                            bk.t[:, :],
                            AA[:, xT_o + dc * S + sc * 128: xT_o + dc * S + sc * 128 + 128],
                            AB[:, off + dc * 512: off + (dc + 1) * 512],
                            start=(dc == 0), stop=(dc == ND - 1)),
                            waits=wl, inc=(dc == ND - 1))
                    vp_freed[bi] = (s_pe, tmm)
                    r = it % 2
                    so = r * 512
                    tev = C.act(lambda e, bk=bk, so=so: e.copy(
                        v_stage[:, so:so + 512], bk.t[:, :]),
                        waits=[(s_pe, tmm), vs_store[r]])
                    bk.free = (s_act, tev)
                    hpe = 512 // 128
                    vs_store[r] = C.stq(m_sv[r],
                                        v_hbm.ap()[b, eb * hpe:(eb + 1) * hpe, :,
                                                   sc * 128:(sc + 1) * 128].rearrange(
                                            "h p e -> p h e"),
                                        v_stage[:, so:so + 512].rearrange(
                                            "p (h e) -> p h e", h=hpe),
                                        waits=[(s_act, tev)])
                    it += 1

            C.barrier()
            qkv_tok = [None, None]
            hb_freed = [None, None]
            a_cons = [None, None, None, None]
            ot_store = [(m, m.n) for m in m_sot]
            lrep_used = [None, None]
            dbg_stok = [None, None, None]
            st_banks = (banks[0], banks[1], banks[6], banks[7])
            av_banks = (banks[2], banks[3])
            l_banks = (banks[4], banks[5])
            iav = [0]
            linv_used = None
            rep_prev = None
            flush_tail = [lambda: None]
            repcopy_prev = None
            it_st = 0
            dbg_ix = [0] * H
            for h in range(H):
                bi = h % 2
                C.ld(m_qkv[bi], AB[:, qb_o + bi * S: qb_o + (bi + 1) * S],
                     qT_hbm.ap()[b, h, :, :], waits=[hb_freed[bi]])
                C.ld(m_qkv[bi], AB[:, kb_o + bi * S: kb_o + (bi + 1) * S],
                     kT_hbm.ap()[b, h, :, :])
                qkv_tok[bi] = C.ld(
                    m_qkv[bi],
                    AB[:, vb_o + bi * S: vb_o + (bi + 1) * S],
                    v_hbm.ap()[b, h, :, :])
                qb = AB[:, qb_o + bi * S: qb_o + (bi + 1) * S]
                kb = AB[:, kb_o + bi * S: kb_o + (bi + 1) * S]
                vb = AB[:, vb_o + bi * S: vb_o + (bi + 1) * S]
                last_pe_this_head = None
                for q4 in range(NSB):
                    nci = 4 * q4 + 4
                    avb = av_banks[iav[0] % 2]
                    l_bank = l_banks[iav[0] % 2]
                    iav[0] += 1
                    atok = {}
                    asem = {}

                    def emit_st(ci):
                        nonlocal it_st, last_pe_this_head
                        bk = st_banks[it_st % 4]
                        tmm = C.pe(lambda e, bk=bk, ci=ci, q4=q4, qb=qb, kb=kb: e.matmul(
                            bk.t[:, :], kb[:, ci * 128:(ci + 1) * 128],
                            qb[:, q4 * 512:(q4 + 1) * 512],
                            start=True, stop=True),
                            waits=[qkv_tok[bi], bk.free])
                        it_st += 1
                        last_pe_this_head = tmm
                        ar = ci % 4
                        j = ci - 4 * q4
                        ao = ab_o + ar * 512
                        if j > 0:
                            tz = C.dve(lambda e, ao=ao, j=j: e.memset(
                                AB[:, ao: ao + 128 * j], 0.0),
                                waits=[(s_pe, a_cons[ar])])
                        te = C.act(lambda e, bk=bk, ao=ao, j=max(j, 0): e.activation(
                            AB[:, ao + 128 * j: ao + 512], bk.t[:, 128 * max(j, 0):],
                            AF.Exp, bias=cvec(NEGC), scale=cvec(SCL)),
                            waits=[(s_pe, tmm), (s_pe, a_cons[ar]),
                                   dbg_stok[ar] if debug else None])
                        bk.free = (s_act, te)
                        if j >= 0:
                            tm = C.dve(lambda e, ao=ao, j=j: e.tensor_mul(
                                AB[:, ao + 128 * j: ao + 128 * j + 128],
                                AB[:, ao + 128 * j: ao + 128 * j + 128],
                                masks[:, 384:512]),
                                waits=[(s_act, te), tk_ini])
                            atok[ci] = (tm, te)
                            asem[ci] = None
                            if debug and b == 0:
                                dbg_stok[ar] = C.stq(
                                    m_dbg, dbgA_t.ap()[h, dbg_ix[h], :, :],
                                    AB[:, ao: ao + 512],
                                    waits=[(s_dve, tm)])
                                dbg_ix[h] += 1
                        else:
                            atok[ci] = te
                            asem[ci] = s_act

                    def emit_lav(ci):
                        nonlocal last_pe_this_head
                        ar = ci % 4
                        if asem[ci] is None:
                            wl = [(s_dve, atok[ci][0]), (s_act, atok[ci][1])]
                        else:
                            wl = [(asem[ci], atok[ci])]
                        if ci == 0:
                            wl.append(l_bank.free)
                        C.pe(lambda e, ci=ci, ar=ar, l_bank=l_bank: e.matmul(
                            l_bank.t[0:1, :], ones_cb[:, :],
                            AB[:, ab_o + ar * 512: ab_o + (ar + 1) * 512],
                            start=(ci == 0), stop=(ci == nci - 1),
                            skip_group_check=True),
                            waits=wl, inc=False)
                        wl2 = [qkv_tok[bi]]
                        if ci == 0:
                            wl2.append(avb.free)
                        tav = C.pe(lambda e, ci=ci, ar=ar, avb=avb, vb=vb: e.matmul(
                            avb.t[:, :], vb[:, ci * 128:(ci + 1) * 128],
                            AB[:, ab_o + ar * 512: ab_o + (ar + 1) * 512],
                            start=(ci == 0), stop=(ci == nci - 1),
                            skip_group_check=True),
                            waits=wl2, inc=True)
                        a_cons[ar] = tav
                        last_pe_this_head = tav
                        return tav

                    emit_st(0)
                    emit_st(1)
                    emit_st(2)
                    tav_last = None
                    for ci in range(3, nci):
                        emit_st(ci)
                        tav_last = emit_lav(ci - 3)
                    tav_last = emit_lav(nci - 3)
                    tav_last = emit_lav(nci - 2)
                    tav_last = emit_lav(nci - 1)
                    flush_tail[0]()
                    t_ln = C.act(lambda e, l_bank=l_bank: e.activation(
                        linv_hif[0:1, :], l_bank.t[0:1, :], AF.Ln),
                        waits=[(s_pe, tav_last), (s_pe, rep_prev),
                               (s_dve, linv_used)])
                    t_rc = C.act(lambda e: e.activation(
                        linv_sb[0:1, :], linv_hif[0:1, :], AF.Exp,
                        scale=consts[0:1, NEG1:NEG1 + 1]),
                        waits=[(s_act, t_ln)])
                    t_hi = C.dve(lambda e: e.tensor_copy(linv_hi[0:1, :], linv_sb[0:1, :]),
                                 waits=[(s_act, t_rc)])
                    t_hf = C.dve(lambda e: e.tensor_copy(linv_hif[0:1, :], linv_hi[0:1, :]),
                                 waits=[(s_dve, t_hi)])
                    t_lo = C.dve(lambda e: e.tensor_sub(linv_lo[0:1, :], linv_sb[0:1, :],
                                                        linv_hif[0:1, :]),
                                 waits=[(s_dve, t_hf)])
                    linv_used = t_lo
                    l_bank.free = (s_act, t_ln)

                    def tail(l_bank=l_bank, avb=avb, t_lo=t_lo, tav_last=tav_last,
                             q4=q4, h=h, b=b):
                        nonlocal repcopy_prev, rep_prev
                        C.pe(lambda e: e.matmul(l_bank.t[:, :], ones_rb[:, :],
                                                linv_hi[0:1, :], start=True, stop=False),
                             waits=[(s_dve, t_lo)], inc=False)
                        t_rep = C.pe(lambda e: e.matmul(l_bank.t[:, :], ones_rb[:, :],
                                                        linv_lo[0:1, :],
                                                        start=False, stop=True))
                        rep_prev = t_rep
                        lr = q4 % 2
                        t_cp = C.dve(lambda e, lr=lr: e.tensor_copy(
                            lrep_sb[:, lr * 512:(lr + 1) * 512], l_bank.t[:, :]),
                            waits=[(s_pe, t_rep), (s_dve, lrep_used[lr])])
                        repcopy_prev = t_cp
                        l_bank.free = (s_dve, t_cp)
                        orr = q4 % 2
                        t_ot = C.dve(lambda e, lr=lr, orr=orr: e.tensor_mul(
                            ot_stage[:, orr * 512:(orr + 1) * 512], avb.t[:, :],
                            lrep_sb[:, lr * 512:(lr + 1) * 512]),
                            waits=[(s_dve, t_cp), ot_store[orr]])
                        lrep_used[lr] = t_ot
                        avb.free = (s_dve, t_ot)
                        ot_store[orr] = C.stq(m_sot[orr],
                                              oT_hbm.ap()[b, q4, :, h * 512:(h + 1) * 512],
                                              ot_stage[:, orr * 512:(orr + 1) * 512],
                                              waits=[(s_dve, t_ot)])

                    flush_tail[0] = tail
                hb_freed[bi] = (s_pe, last_pe_this_head)
            flush_tail[0]()

            C.barrier()
            ln_state = {}
            gy_tok = []
            sy_tok = []
            otg_tok = [None, None]
            otg_freed = [None, None]
            wop_tok = [None, None]
            wop_freed = [None, None]
            xc_tok = [None, None]
            xc_freed = [None, None]
            itb = 0
            for g in range(NS // 4):
                gb = g % 2
                otg_tok[gb] = C.ld(
                    m_otg[gb],
                    AB[:, otg_o + gb * H * 512: otg_o + (gb + 1) * H * 512],
                    oT_hbm.ap()[b, g, :, :],
                    waits=[otg_freed[gb]])
                otg = AB[:, otg_o + gb * H * 512: otg_o + (gb + 1) * H * 512]
                for eb in range(NEB):
                    wb_i = (g * NEB + eb) % 2
                    woff = wop_o + wb_i * H * 512
                    wop_tok[wb_i] = C.ld(
                        m_wop[wb_i],
                        AB[:, woff:woff + H * 512],
                        wo_bf.ap()[eb * 128:(eb + 1) * 128, :],
                        waits=[tk_wo, wop_freed[wb_i]])
                    for si in range(4):
                        sc = g * 4 + si
                        xr = itb % 2
                        xc_tok[xr] = C.ld(m_xc[xr], xc_sb[:, xr * 512:(xr + 1) * 512],
                                          xap[b, sc * 128:(sc + 1) * 128,
                                              eb * 512:(eb + 1) * 512],
                                          waits=[xc_freed[xr]])
                        bk = banks[itb % 6]
                        for h in range(H):
                            wl = []
                            if h == 0:
                                wl = [otg_tok[gb], wop_tok[wb_i], bk.free]
                            tmm = C.pe(lambda e, bk=bk, h=h, si=si, woff=woff, otg=otg: e.matmul(
                                bk.t[:, :],
                                otg[:, h * 512 + si * 128: h * 512 + si * 128 + 128],
                                AB[:, woff + h * 512: woff + (h + 1) * 512],
                                start=(h == 0), stop=(h == H - 1)),
                                waits=wl, inc=(h == H - 1))
                        wop_freed[wb_i] = (s_pe, tmm)
                        otg_freed[gb] = (s_pe, tmm)
                        p4_last_tok = tmm
                        tev = C.dve(lambda e, bk=bk, si=si, eb=eb, xr=xr: e.tensor_add(
                            FA[:, r1_o + si * D + eb * 512: r1_o + si * D + (eb + 1) * 512],
                            bk.t[:, :], xc_sb[:, xr * 512:(xr + 1) * 512]),
                            waits=[(s_pe, tmm), xc_tok[xr],
                                   (m_sy[si], m_sy[si].n), (m_gy[si], m_gy[si].n)])
                        bk.free = (s_dve, tev)
                        xc_freed[xr] = (s_dve, tev)
                        itb += 1
                rdy = [(s_dve, s_dve.n)]
                for si in range(4):
                    sc = g * 4 + si
                    r_ap = FA[:, r1_o + si * D: r1_o + (si + 1) * D]
                    t_y = layernorm(r_ap, ln_state, rdy=rdy)
                    C.stq(m_sy[si], y_hbm.ap()[b, sc * 128:(sc + 1) * 128, :], r_ap,
                          waits=[(s_dve, t_y)])
                    C.gpd(m_gy[si], yb_hbm.ap()[b, sc * 128:(sc + 1) * 128, :], r_ap,
                          waits=[(s_dve, t_y)])
                gy_tok.append([(m, m.n) for m in m_gy])
                sy_tok.append([(m, m.n) for m in m_sy])

            p4_last = (s_pe, p4_last_tok)
            p4_ev = [xc_freed[0], xc_freed[1]]
            w1_tok = [None, None]
            w1_freed = [p4_last, p4_last]
            w2_tok = [None, None]
            w2_freed = [p4_last, p4_last]
            w3_tok = [None, None]
            w3_freed = [None, None]
            yl_tok = [None, None]
            yl_freed = [p4_ev[0], p4_ev[1]]
            ln2_prev = ln_state
            f_last_prev = p4_last[1]
            h1_last_prev = p4_last[1]
            h1_last_prev = otg_freed[0][1]
            for t5 in range(NT):
                row0 = t5 * STILE
                for dc in range(ND):
                    wl_yt = [(s_pe, h1_last_prev)] + gy_tok[t5]
                    tk_yT = C.ld(m_yt, AB[:, yT_o + dc * STILE: yT_o + (dc + 1) * STILE],
                                 yb_hbm.ap()[b, row0:row0 + STILE,
                                             dc * 128:(dc + 1) * 128],
                                 waits=wl_yt, transpose=True)
                for j in range(NE1):
                    wb_i = j % 2
                    woff = w1c_o + wb_i * ND * 128
                    w1_tok[wb_i] = C.ld(
                        m_w1[wb_i],
                        AB[:, woff:woff + ND * 128],
                        w1_bf.ap()[j * 128:(j + 1) * 128, :],
                        waits=[tk_w1, w1_freed[wb_i]])
                    bk = banks[j % 4]
                    for dc in range(ND):
                        wl = []
                        if dc == 0:
                            wl = [w1_tok[wb_i], tk_yT, bk.free]
                        tmm = C.pe(lambda e, bk=bk, woff=woff, dc=dc: e.matmul(
                            bk.t[:, 0:STILE],
                            AB[:, woff + dc * 128: woff + (dc + 1) * 128],
                            AB[:, yT_o + dc * STILE: yT_o + (dc + 1) * STILE],
                            start=(dc == 0), stop=(dc == ND - 1)),
                            waits=wl, inc=(dc == ND - 1))
                    w1_freed[wb_i] = (s_pe, tmm)
                    tev = C.act(lambda e, bk=bk, j=j: e.activation(
                        AA[:, h1_o + j * STILE: h1_o + (j + 1) * STILE],
                        bk.t[:, 0:STILE], AF.Relu, bias=b1t_sb[:, j:j + 1]),
                        waits=[(s_pe, tmm), tk_ini, (s_pe, f_last_prev),
                               (m_w3[1], m_w3[1].n)])
                    bk.free = (s_act, tev)
                h1_last_prev = w1_freed[0][1] if w1_freed[0] else None
                h1_last_prev = max(h1_last_prev or 0,
                                   w1_freed[1][1] if w1_freed[1] else 0) or None
                tk_h1_done = C.sems["act"].n
                for j2 in range(NE2):
                    wb_i = j2 % 2
                    woff = w2c_o + wb_i * NE1 * 128
                    w2_tok[wb_i] = C.ld(
                        m_w2[wb_i],
                        AB[:, woff:woff + NE1 * 128],
                        w2_bf.ap()[j2 * 128:(j2 + 1) * 128, :],
                        waits=[tk_w2, w2_freed[wb_i]])
                    bk = banks[4 + (j2 % 4)]
                    for ec in range(NE1):
                        wl = []
                        if ec == 0:
                            wl = [w2_tok[wb_i], (s_act, tk_h1_done), bk.free]
                        tmm = C.pe(lambda e, bk=bk, woff=woff, ec=ec: e.matmul(
                            bk.t[:, 0:STILE],
                            AB[:, woff + ec * 128: woff + (ec + 1) * 128],
                            AA[:, h1_o + ec * STILE: h1_o + (ec + 1) * STILE],
                            start=(ec == 0), stop=(ec == NE1 - 1)),
                            waits=wl, inc=(ec == NE1 - 1))
                    w2_freed[wb_i] = (s_pe, tmm)
                    tev = C.act(lambda e, bk=bk, j2=j2: e.activation(
                        AB[:, h2_off(j2): h2_off(j2) + STILE],
                        bk.t[:, 0:STILE], AF.Relu, bias=b2t_sb[:, j2:j2 + 1]),
                        waits=[(s_pe, tmm), tk_ini])
                    bk.free = (s_act, tev)
                tk_h2_done = C.sems["act"].n
                itf = 0
                for eb in range(NEB):
                    wb_i = eb % 2
                    woff = w3slotA if wb_i == 0 else h1_o
                    wl_extra = []
                    if wb_i == 1:
                        wl_extra = [w2_freed[0], w2_freed[1]]
                    w3_tok[wb_i] = C.ld(
                        m_w3[wb_i],
                        AA[:, woff:woff + NE2 * 512],
                        w3_bf.ap()[eb * 128:(eb + 1) * 128, :],
                        waits=[tk_w3, w3_freed[wb_i]] + wl_extra)
                    for si in range(STILE // 128):
                        yr = itf % 2
                        yl_tok[yr] = C.ld(m_yl[yr], xc_sb[:, yr * 512:(yr + 1) * 512],
                                          y_hbm.ap()[b, row0 + si * 128: row0 + (si + 1) * 128,
                                                     eb * 512:(eb + 1) * 512],
                                          waits=[yl_freed[yr]] + sy_tok[t5])
                        bk = banks[itf % 4]
                        for ec in range(NE2):
                            wl = []
                            if ec == 0:
                                wl = [w3_tok[wb_i], (s_act, tk_h2_done), bk.free]
                            C.pe(lambda e, bk=bk, woff=woff, ec=ec, si=si: e.matmul(
                                bk.t[:, :],
                                AB[:, h2_off(ec) + si * 128: h2_off(ec) + si * 128 + 128],
                                AA[:, woff + ec * 512: woff + (ec + 1) * 512],
                                start=(ec == 0), stop=False),
                                waits=wl, inc=False)
                        tmm = C.pe(lambda e, bk=bk, eb=eb: e.matmul(
                            bk.t[:, :], ones_rb[:, :],
                            b3b_sb[0:1, eb * 512:(eb + 1) * 512],
                            start=False, stop=True),
                            waits=[(s_dve, tk_b3b)], inc=True)
                        w3_freed[wb_i] = (s_pe, tmm)
                        f_last = (s_pe, tmm)
                        tev = C.dve(lambda e, bk=bk, si=si, eb=eb, yr=yr: e.tensor_add(
                            FA[:, r1_o + si * D + eb * 512: r1_o + si * D + (eb + 1) * 512],
                            bk.t[:, :], xc_sb[:, yr * 512:(yr + 1) * 512]),
                            waits=[(s_pe, tmm), yl_tok[yr],
                                   (s_act, ln2_prev.get("sqrt")),
                                   (m_sout, m_sout.n),
                                   (m_sy[si], m_sy[si].n), (m_gy[si], m_gy[si].n)])
                        bk.free = (s_dve, tev)
                        yl_freed[yr] = (s_dve, tev)
                        itf += 1
                f_last_prev = f_last[1]
                w1_freed = [f_last, f_last]
                rdy2 = [(s_dve, s_dve.n)]
                for si in range(STILE // 128):
                    r_ap = FA[:, r1_o + si * D: r1_o + (si + 1) * D]
                    t_y = layernorm(r_ap, ln2_prev, rdy=rdy2)
                    C.stq(m_sout, outap[b, row0 + si * 128: row0 + (si + 1) * 128, :],
                          r_ap, waits=[(s_dve, t_y)])
            prev_f_last = f_last_prev
            prev_ln_sqrt = ln_state.get("sqrt")

        C.barrier()
        fin = [(s, s.n) for s in C.all_sems]

        def final_waits(e):
            for s, v in fin:
                if v > 0:
                    e.wait_ge(s.h, v)

        C.ops["sp"].append(final_waits)

        with nc.Block() as blk:
            blk.tensor(lambda e: [f(e) for f in C.ops["pe"]])
            blk.vector(lambda e: [f(e) for f in C.ops["dve"]])
            blk.scalar(lambda e: [f(e) for f in C.ops["act"]])
            blk.gpsimd(lambda e: [f(e) for f in C.ops["gp"]])
            blk.sync(lambda e: [f(e) for f in C.ops["sp"]])
    return nc


def make_consts(cfg):
    S = cfg.S
    inv_freq = 1.0 / (10000.0 ** (np.arange(0, 128, 2, dtype=np.float64) / 128.0))
    t = np.arange(S, dtype=np.float64)
    freqs = np.outer(t, inv_freq)
    cosf = np.cos(freqs).T.astype(np.float32)
    sinf = np.sin(freqs).T.astype(np.float32)
    cos_tab = np.concatenate([cosf, cosf], axis=0)
    sin_tab = np.concatenate([-sinf, sinf], axis=0)
    r = np.arange(128)[:, None]
    u = np.arange(896)[None, :]
    masks = ((u - 384) >= r).astype(np.float32).astype(ml_dtypes.bfloat16)
    return cos_tab, sin_tab, masks


def _panel(wT, nin, nout, ocols):
    return np.ascontiguousarray(
        wT.reshape(nin, 128, nout, ocols).transpose(2, 1, 0, 3)
          .reshape(nout * 128, nin * ocols))


def make_inputs_for_core(cfg, x_shard, Wq, Wk, Wv, Wo, W1, b1, W2, b2, W3, b3):
    cos_tab, sin_tab, masks = make_consts(cfg)
    c = np.ascontiguousarray
    ND, NEB, NE1, NE2, H = cfg.ND, cfg.NEB, cfg.NE1, cfg.NE2, cfg.H
    return {
        "x": c(x_shard.astype(np.float32)),
        "wqT": _panel(Wq.T, ND, H, 128),
        "wkT": _panel(Wk.T, ND, H, 128),
        "wvT": _panel(Wv.T, ND, NEB, 512),
        "woT": _panel(Wo.T, H, NEB, 512),
        "w1T": _panel(W1.T, ND, NE1, 128),
        "w2T": _panel(W2.T, NE1, NE2, 128),
        "w3T": _panel(W3.T, NE2, NEB, 512),
        "b1t": c(b1.reshape(cfg.NE1, 128).T),
        "b2t": c(b2.reshape(cfg.NE2, 128).T),
        "b3row": c(b3.reshape(1, cfg.D)),
        "cos_t": cos_tab, "sin_t": sin_tab, "masks": masks,
    }


def kernel(L0_attn_x, L0_attn_Wq, L0_attn_Wk, L0_attn_Wv, L0_attn_Wo,
           L0_ffn_W1, L0_ffn_b1, L0_ffn_W2, L0_ffn_b2, L0_ffn_W3, L0_ffn_b3):
    cfg = FULL
    n_cores = 8
    x = np.asarray(L0_attn_x, dtype=np.float32)
    args = [np.asarray(a, dtype=np.float32) for a in
            (L0_attn_Wq, L0_attn_Wk, L0_attn_Wv, L0_attn_Wo,
             L0_ffn_W1, L0_ffn_b1, L0_ffn_W2, L0_ffn_b2, L0_ffn_W3, L0_ffn_b3)]
    nc = build_program(cfg)
    in_maps = []
    base = None
    for i in range(n_cores):
        shard = x[i * cfg.B:(i + 1) * cfg.B]
        if base is None:
            base = make_inputs_for_core(cfg, shard, *args)
            m = base
        else:
            m = dict(base)
            m["x"] = np.ascontiguousarray(shard)
        in_maps.append(m)
    res = run_bass_kernel_spmd(nc, in_maps, core_ids=list(range(n_cores)))
    return np.concatenate([r["out"] for r in res.results], axis=0)


# version 22 (final)
# speedup vs baseline: 1.0742x; 1.0742x over previous
import numpy as np
import ml_dtypes
from contextlib import ExitStack

import concourse.bass as bass
import concourse.mybir as mybir
from concourse.bass_utils import run_bass_kernel_spmd

F32 = mybir.dt.float32
BF16 = mybir.dt.bfloat16
AF = mybir.ActivationFunctionType
ALU = mybir.AluOpType
AX = mybir.AxisListType

SCALE = 1.0 / float(np.sqrt(128.0))
CSHIFT = 16.0
EPS = 1e-5


class Cfg:
    def __init__(self, B, S, D, DFF):
        self.B = B
        self.S = S
        self.D = D
        self.DFF = DFF
        self.H = D // 128
        self.ND = D // 128
        self.NS = S // 128
        self.NSB = S // 512
        self.NEB = D // 512
        self.NE1 = DFF // 128
        self.NE2 = DFF // 128
        self.STILE = 512
        self.NT = S // self.STILE


FULL = Cfg(B=2, S=2048, D=2048, DFF=4096)


class Sem:
    __slots__ = ("h", "n", "name")


class Bank:
    __slots__ = ("t", "free")

    def __init__(self, t):
        self.t = t
        self.free = None


class Ctx:
    ENGS = ("pe", "act", "dve", "sp", "gp")

    def __init__(self, nc, stack):
        self.nc = nc
        self.stack = stack
        self.ops = {e: [] for e in self.ENGS}
        self.waited = {e: {} for e in self.ENGS}
        self.all_sems = []
        self.no_barrier = set()
        self.sems = {}
        for name in ("pe", "act", "dve"):
            self.sems[name] = self.new_sem(name)

    def new_sem(self, name, no_barrier=False):
        s = Sem()
        s.h = self.stack.enter_context(self.nc.semaphore("sm_" + name))
        s.n = 0
        s.name = name
        self.all_sems.append(s)
        if no_barrier:
            self.no_barrier.add(name)
        return s

    def emit(self, eng, fn, waits=(), inc=None, inck=1):
        lst = self.ops[eng]
        wt = self.waited[eng]
        for w in waits:
            if w is None:
                continue
            s, v = w
            if v is None or v <= 0:
                continue
            if wt.get(s.name, 0) >= v:
                continue
            wt[s.name] = v
            lst.append(lambda e, s=s, v=v: e.wait_ge(s.h, v))
        if inc is not None:
            inc.n += inck
            tok = inc.n

            def run(e, fn=fn, inc=inc, k=inck):
                fn(e).then_inc(inc.h, k)

            lst.append(run)
            return tok
        lst.append(fn)
        return None

    def pe(self, fn, waits=(), inc=True):
        return self.emit("pe", fn, waits, self.sems["pe"] if inc else None)

    def act(self, fn, waits=(), inc=True):
        return self.emit("act", fn, waits, self.sems["act"] if inc else None)

    def dve(self, fn, waits=(), inc=True):
        return self.emit("dve", fn, waits, self.sems["dve"] if inc else None)

    def ld(self, sem, out, in_, waits=(), transpose=False):
        if transpose:
            fn = lambda e: e.dma_start(out, in_, transpose=True)
        else:
            fn = lambda e: e.dma_start(out, in_)
        self.emit("sp", fn, waits, sem, 16)
        return (sem, sem.n)

    def stq(self, sem, out, in_, waits=()):
        fn = lambda e: e.dma_start(out, in_)
        self.emit("act", fn, waits, sem, 16)
        return (sem, sem.n)

    def gpd(self, sem, out, in_, waits=()):
        fn = lambda e: e.dma_start(out, in_)
        self.emit("gp", fn, waits, sem, 16)
        return (sem, sem.n)

    def barrier(self):
        toks = [(s, s.n) for s in self.all_sems if s.name not in self.no_barrier]
        for e in self.ENGS:
            for s, v in toks:
                if v <= 0 or self.waited[e].get(s.name, 0) >= v:
                    continue
                self.waited[e][s.name] = v
                self.ops[e].append(lambda eng, s=s, v=v: eng.wait_ge(s.h, v))


def build_program(cfg, debug=False):
    nc = bass.Bass("TRN2", target_bir_lowering=False,
                   dynamic_dma_scratch_size=4096)
    DBG = "ExternalOutput" if debug else "Internal"
    B, S, D, DFF = cfg.B, cfg.S, cfg.D, cfg.DFF
    H, ND, NS, NSB, NEB = cfg.H, cfg.ND, cfg.NS, cfg.NSB, cfg.NEB
    NE1, NE2, STILE, NT = cfg.NE1, cfg.NE2, cfg.STILE, cfg.NT

    x_t = nc.dram_tensor("x", [B, S, D], F32, kind="ExternalInput")
    wqT_t = nc.dram_tensor("wqT", [H * 128, ND * 128], F32, kind="ExternalInput")
    wkT_t = nc.dram_tensor("wkT", [H * 128, ND * 128], F32, kind="ExternalInput")
    wvT_t = nc.dram_tensor("wvT", [NEB * 128, ND * 512], F32, kind="ExternalInput")
    woT_t = nc.dram_tensor("woT", [NEB * 128, H * 512], F32, kind="ExternalInput")
    w1T_t = nc.dram_tensor("w1T", [NE1 * 128, ND * 128], F32, kind="ExternalInput")
    w2T_t = nc.dram_tensor("w2T", [NE2 * 128, NE1 * 128], F32, kind="ExternalInput")
    w3T_t = nc.dram_tensor("w3T", [NEB * 128, NE2 * 512], F32, kind="ExternalInput")
    b1t_t = nc.dram_tensor("b1t", [128, NE1], F32, kind="ExternalInput")
    b2t_t = nc.dram_tensor("b2t", [128, NE2], F32, kind="ExternalInput")
    b3r_t = nc.dram_tensor("b3row", [1, D], F32, kind="ExternalInput")
    cos_t = nc.dram_tensor("cos_t", [128, S], F32, kind="ExternalInput")
    sin_t = nc.dram_tensor("sin_t", [128, S], F32, kind="ExternalInput")
    mask_t = nc.dram_tensor("masks", [128, 896], BF16, kind="ExternalInput")
    out_t = nc.dram_tensor("out", [B, S, D], F32, kind="ExternalOutput")

    wq_bf = nc.dram_tensor("wq_bf", [H * 128, ND * 128], BF16, kind="Internal")
    wk_bf = nc.dram_tensor("wk_bf", [H * 128, ND * 128], BF16, kind="Internal")
    wv_bf = nc.dram_tensor("wv_bf", [NEB * 128, ND * 512], BF16, kind="Internal")
    wo_bf = nc.dram_tensor("wo_bf", [NEB * 128, H * 512], BF16, kind="Internal")
    w1_bf = nc.dram_tensor("w1_bf", [NE1 * 128, ND * 128], BF16, kind="Internal")
    w2_bf = nc.dram_tensor("w2_bf", [NE2 * 128, NE1 * 128], BF16, kind="Internal")
    w3_bf = nc.dram_tensor("w3_bf", [NEB * 128, NE2 * 512], BF16, kind="Internal")
    xb_hbm = nc.dram_tensor("xb_hbm", [B, S, D], BF16, kind="Internal")
    qT_hbm = nc.dram_tensor("qT_hbm", [B, H, 128, S], BF16, kind=DBG)
    kT_hbm = nc.dram_tensor("kT_hbm", [B, H, 128, S], BF16, kind=DBG)
    v_hbm = nc.dram_tensor("v_hbm", [B, H, 128, NS * 128], BF16, kind=DBG)
    oT_hbm = nc.dram_tensor("oT_hbm", [B, NSB, 128, H * 512], BF16, kind=DBG)
    y_hbm = nc.dram_tensor("y_hbm", [B, S, D], F32, kind=DBG)
    yb_hbm = nc.dram_tensor("yb_hbm", [B, S, D], BF16, kind="Internal")
    dbgA_t = None
    if debug:
        NTIL = 2 * NSB * NSB + 2 * NSB
        dbgA_t = nc.dram_tensor("dbgA", [H, NTIL, 128, 512], BF16,
                                kind="ExternalOutput")

    xap = x_t.ap()
    outap = out_t.ap()

    st = ExitStack()
    with st:
        C = Ctx(nc, st)
        s_pe, s_act, s_dve = C.sems["pe"], C.sems["act"], C.sems["dve"]

        m_init = C.new_sem("ini")
        m_cw = {nm: C.new_sem("c" + nm, no_barrier=True)
                for nm in ("x0", "x1", "wq", "wk", "wv", "wo", "w1", "w2", "w3")}
        m_xt = C.new_sem("xt")
        m_qkp = [C.new_sem("qkp0"), C.new_sem("qkp1")]
        m_wv = [C.new_sem("wv0"), C.new_sem("wv1")]
        m_qkv = [C.new_sem("qkv0"), C.new_sem("qkv1")]
        m_otg = [C.new_sem("otg0"), C.new_sem("otg1")]
        m_wop = [C.new_sem("wop0"), C.new_sem("wop1")]
        m_xc = [C.new_sem("xc0"), C.new_sem("xc1")]
        m_yt = C.new_sem("yt")
        m_w1 = [C.new_sem("w10"), C.new_sem("w11")]
        m_w2 = [C.new_sem("w20"), C.new_sem("w21")]
        m_w3 = [C.new_sem("w30"), C.new_sem("w31")]
        m_yl = [C.new_sem("yl0"), C.new_sem("yl1")]
        m_sqk = [C.new_sem("sqk0"), C.new_sem("sqk1")]
        m_sv = [C.new_sem("sv0"), C.new_sem("sv1")]
        m_sot = [C.new_sem("sot0"), C.new_sem("sot1")]
        m_sy = [C.new_sem(f"sy{i}") for i in range(4)]
        m_sout = C.new_sem("sout")
        m_gy = [C.new_sem(f"gy{i}") for i in range(4)]
        m_dbg = C.new_sem("dbg", no_barrier=True)

        WA = max(ND * S, NE1 * STILE + NE2 * 512)
        NCH1 = (2 * ND * 128) // 512
        WB = max(2 * ND * 128 + 2 * ND * 512,
                 6 * S + 4 * 512,
                 4 * H * 512,
                 ND * STILE + 2 * ND * 128 + 2 * NE1 * 128 + (NE2 - NCH1) * 512)
        WF = max(2 * S + 4 * 512, 5 * D)
        AA = nc.alloc_sbuf_tensor("arena_a", [128, WA], BF16)
        AB = nc.alloc_sbuf_tensor("arena_b", [128, WB], BF16)
        FA = nc.alloc_sbuf_tensor("farena", [128, WF], F32)
        masks = nc.alloc_sbuf_tensor("masks_sb", [128, 896], BF16)
        stats = nc.alloc_sbuf_tensor("stats", [128, 8], F32)
        consts = nc.alloc_sbuf_tensor("consts", [128, 4], F32)
        b1t_sb = nc.alloc_sbuf_tensor("b1t_sb", [128, NE1], F32)
        b2t_sb = nc.alloc_sbuf_tensor("b2t_sb", [128, NE2], F32)
        b3f_sb = nc.alloc_sbuf_tensor("b3f_sb", [1, D], F32)
        b3b_sb = nc.alloc_sbuf_tensor("b3b_sb", [1, D], BF16)
        ones_cb = nc.alloc_sbuf_tensor("ones_cb", [128, 1], BF16)
        ones_rb = nc.alloc_sbuf_tensor("ones_rb", [1, 128], BF16)
        ones_rf = nc.alloc_sbuf_tensor("ones_rf", [1, 128], F32)
        linv_sb = nc.alloc_sbuf_tensor("linv_sb", [1, 512], F32)
        linv_hi = nc.alloc_sbuf_tensor("linv_hi", [1, 512], BF16)
        linv_hif = nc.alloc_sbuf_tensor("linv_hif", [1, 512], F32)
        linv_lo = nc.alloc_sbuf_tensor("linv_lo", [1, 512], BF16)
        lrep_sb = nc.alloc_sbuf_tensor("lrep_sb", [128, 2 * 512], F32)
        cos_sb = nc.alloc_sbuf_tensor("cos_sb", [128, S], F32)
        sin_sb = nc.alloc_sbuf_tensor("sin_sb", [128, S], F32)
        qk_stage = nc.alloc_sbuf_tensor("qk_stage", [128, 2 * 512], BF16)
        v_stage = nc.alloc_sbuf_tensor("v_stage", [128, 2 * 512], BF16)
        ot_stage = nc.alloc_sbuf_tensor("ot_stage", [128, 2 * 512], BF16)
        xc_sb = nc.alloc_sbuf_tensor("xc_sb", [128, 2 * 512], F32)

        banks = [Bank(st.enter_context(nc.psum_tensor(f"pb{i}", [128, 512], F32)))
                 for i in range(8)]

        NEGC, SCL, EPSC = 0, 1, 2
        cvec = lambda i: consts[:, i:i + 1]

        C.dve(lambda e: e.memset(consts[:, NEGC:NEGC + 1], -CSHIFT), inc=False)
        C.dve(lambda e: e.memset(consts[:, SCL:SCL + 1], SCALE), inc=False)
        C.dve(lambda e: e.memset(consts[:, EPSC:EPSC + 1], EPS), inc=False)
        C.dve(lambda e: e.memset(ones_cb[:, :], 1.0), inc=False)
        C.dve(lambda e: e.memset(ones_rb[:, :], 1.0), inc=False)
        C.dve(lambda e: e.memset(ones_rf[:, :], 1.0))
        C.ld(m_init, masks[:, :], mask_t.ap()[:, :])
        C.ld(m_init, cos_sb[:, :], cos_t.ap()[:, :])
        C.ld(m_init, sin_sb[:, :], sin_t.ap()[:, :])
        C.ld(m_init, b1t_sb[:, :], b1t_t.ap()[:, :])
        C.ld(m_init, b2t_sb[:, :], b2t_t.ap()[:, :])
        tk_ini = C.ld(m_init, b3f_sb[:, :], b3r_t.ap()[:, :])
        tk_b3b = C.dve(lambda e: e.tensor_copy(b3b_sb[:, :], b3f_sb[:, :]),
                       waits=[tk_ini])

        def cast_dram(sem, dst_t, src_t, rows, chunk_rows):
            tok = None
            r = 0
            while r < rows:
                n = min(chunk_rows, rows - r)
                tok = C.gpd(sem, dst_t.ap()[r:r + n, :], src_t.ap()[r:r + n, :])
                r += n
            return tok

        def cast_x(b):
            for r in range(0, S, 512):
                C.gpd(m_cw[f"x{b}"], xb_hbm.ap()[b, r:r + 512, :],
                      xap[b, r:r + 512, :])
            return (m_cw[f"x{b}"], m_cw[f"x{b}"].n)

        tk_xcast = [None] * B
        tk_xcast[0] = cast_x(0)
        tk_wq = cast_dram(m_cw["wq"], wq_bf, wqT_t, H * 128, 1024)
        tk_wk = cast_dram(m_cw["wk"], wk_bf, wkT_t, H * 128, 1024)
        tk_wv = cast_dram(m_cw["wv"], wv_bf, wvT_t, NEB * 128, 256)
        if B > 1:
            tk_xcast[1] = cast_x(1)
        tk_wo = cast_dram(m_cw["wo"], wo_bf, woT_t, NEB * 128, 256)
        tk_w1 = cast_dram(m_cw["w1"], w1_bf, w1T_t, NE1 * 128, 1024)
        tk_w2 = cast_dram(m_cw["w2"], w2_bf, w2T_t, NE2 * 128, 512)
        tk_w3 = cast_dram(m_cw["w3"], w3_bf, w3T_t, NEB * 128, 128)

        xT_o = 0
        h1_o = 0
        w3slotA = NE1 * STILE
        qkc_o = 0
        wvs_o = 2 * ND * 128
        qb_o = 0
        kb_o = 2 * S
        vb_o = 4 * S
        ab_o = 6 * S
        otg_o = 0
        wop_o = 2 * H * 512
        yT_o = 0
        w1c_o = ND * STILE
        w2c_o = ND * STILE + 2 * ND * 128
        h2_base2 = ND * STILE + 2 * ND * 128 + 2 * NE1 * 128

        def h2_off(j2):
            if j2 < NCH1:
                return w1c_o + j2 * 512
            return h2_base2 + (j2 - NCH1) * 512

        tmp_o = 0
        u_o = 2 * 512
        r1_o = 0
        sq_o = 4 * D

        def layernorm(r_ap, prev, rdy=()):
            w = [(s_dve, prev.get("apply")), (s_act, prev.get("sqrt"))] + list(rdy)
            t_sum = C.dve(lambda e: e.reduce_sum(stats[:, 0:1], r_ap, axis=AX.X),
                          waits=w)
            t_sq = C.act(lambda e: e.activation(FA[:, sq_o:sq_o + D], r_ap, AF.Square,
                                                accum_out=stats[:, 1:2]),
                         waits=[(s_dve, prev.get("var"))] + list(rdy))
            t_nm = C.dve(lambda e: e.tensor_scalar(stats[:, 2:3], stats[:, 0:1],
                                                   -1.0 / D, None, op0=ALU.mult),
                         waits=[(s_dve, t_sum)])
            t_v1 = C.dve(lambda e: e.tensor_scalar(stats[:, 3:4], stats[:, 1:2],
                                                   1.0 / D, None, op0=ALU.mult),
                         waits=[(s_act, t_sq)])
            t_m2 = C.dve(lambda e: e.tensor_mul(stats[:, 6:7], stats[:, 2:3],
                                                stats[:, 2:3]),
                         waits=[(s_dve, t_nm)])
            t_var = C.dve(lambda e: e.tensor_sub(stats[:, 3:4], stats[:, 3:4],
                                                 stats[:, 6:7]),
                          waits=[(s_dve, t_m2)])
            t_sd = C.act(lambda e: e.activation(stats[:, 4:5], stats[:, 3:4], AF.Sqrt,
                                                bias=cvec(EPSC)),
                         waits=[(s_dve, t_var)])
            t_rs = C.dve(lambda e: e.reciprocal(stats[:, 5:6], stats[:, 4:5]),
                         waits=[(s_act, t_sd)])
            t_ap = C.dve(lambda e: e.tensor_scalar(r_ap, r_ap, stats[:, 2:3],
                                                   stats[:, 5:6], op0=ALU.add,
                                                   op1=ALU.mult),
                         waits=[(s_dve, t_rs)])
            prev["apply"] = t_ap
            prev["sqrt"] = t_sd
            prev["var"] = t_var
            return t_ap

        prev_f_last = None
        prev_ln_sqrt = None
        for b in range(B):
            if b == 0:
                C.barrier()
            for dc in range(ND):
                tk_xT = C.ld(m_xt, AA[:, xT_o + dc * S: xT_o + (dc + 1) * S],
                             xb_hbm.ap()[b, :, dc * 128:(dc + 1) * 128],
                             waits=[tk_xcast[b], (s_pe, prev_f_last)], transpose=True)

            if b == 0:
                C.barrier()
            pan_tok = [None, None]
            pan_freed = [None, None]
            stage_store = [(m, m.n) for m in m_sqk]
            tmp_done = [None, None]
            it = 0
            for (wsrc, wtok, dst) in ((wq_bf, tk_wq, qT_hbm), (wk_bf, tk_wk, kT_hbm)):
                for h in range(H):
                    bi = h % 2
                    off = qkc_o + bi * ND * 128
                    pan_tok[bi] = C.ld(
                        m_qkp[bi],
                        AB[:, off:off + ND * 128],
                        wsrc.ap()[h * 128:(h + 1) * 128, :],
                        waits=[wtok, pan_freed[bi], (s_pe, prev_f_last)])
                    for sb in range(NSB):
                        bk = banks[it % 2]
                        for dc in range(ND):
                            wl = []
                            if dc == 0:
                                wl = [pan_tok[bi], tk_xT, bk.free]
                            tmm = C.pe(lambda e, bk=bk, off=off, dc=dc, sb=sb: e.matmul(
                                bk.t[:, :],
                                AB[:, off + dc * 128: off + (dc + 1) * 128],
                                AA[:, xT_o + dc * S + sb * 512: xT_o + dc * S + (sb + 1) * 512],
                                start=(dc == 0), stop=(dc == ND - 1)),
                                waits=wl, inc=(dc == ND - 1))
                        pan_freed[bi] = (s_pe, tmm)
                        r = it % 2
                        to = tmp_o + r * 512
                        uo = u_o + r * 512
                        so = r * 512
                        cs = cos_sb[:, sb * 512:(sb + 1) * 512]
                        sn = sin_sb[:, sb * 512:(sb + 1) * 512]
                        C.dve(lambda e, bk=bk, to=to, sn=sn: e.tensor_tensor(
                            FA[0:64, to:to + 512], bk.t[64:128, :], sn[0:64, :], op=ALU.mult),
                            waits=[(s_pe, tmm), (s_dve, tmp_done[r]), tk_ini,
                                   (s_act, prev_ln_sqrt), (m_sout, m_sout.n)])
                        C.dve(lambda e, bk=bk, to=to, sn=sn: e.tensor_tensor(
                            FA[64:128, to:to + 512], bk.t[0:64, :], sn[64:128, :], op=ALU.mult))
                        tc = C.dve(lambda e, bk=bk, uo=uo, cs=cs: e.tensor_tensor(
                            FA[:, uo:uo + 512], bk.t[:, :], cs[:, :], op=ALU.mult))
                        bk.free = (s_dve, tc)
                        td = C.dve(lambda e, to=to, uo=uo, so=so: e.tensor_add(
                            qk_stage[:, so:so + 512], FA[:, uo:uo + 512], FA[:, to:to + 512]),
                            waits=[(s_dve, tc), stage_store[r]])
                        tmp_done[r] = td
                        stage_store[r] = C.stq(m_sqk[r],
                                               dst.ap()[b, h, :, sb * 512:(sb + 1) * 512],
                                               qk_stage[:, so:so + 512],
                                               waits=[(s_dve, td)])
                        it += 1
            vp_tok = [None, None]
            vp_freed = [None, None]
            vs_store = [(m, m.n) for m in m_sv]
            it = 0
            for eb in range(NEB):
                bi = eb % 2
                off = wvs_o + bi * ND * 512
                vp_tok[bi] = C.ld(
                    m_wv[bi],
                    AB[:, off:off + ND * 512],
                    wv_bf.ap()[eb * 128:(eb + 1) * 128, :],
                    waits=[tk_wv, vp_freed[bi], (s_pe, prev_f_last)])
                for sc in range(NS):
                    bk = banks[2 + (it % 2)]
                    for dc in range(ND):
                        wl = []
                        if dc == 0:
                            wl = [vp_tok[bi], bk.free]
                        tmm = C.pe(lambda e, bk=bk, off=off, dc=dc, sc=sc: e.matmul(
                            bk.t[:, :],
                            AA[:, xT_o + dc * S + sc * 128: xT_o + dc * S + sc * 128 + 128],
                            AB[:, off + dc * 512: off + (dc + 1) * 512],
                            start=(dc == 0), stop=(dc == ND - 1)),
                            waits=wl, inc=(dc == ND - 1))
                    vp_freed[bi] = (s_pe, tmm)
                    r = it % 2
                    so = r * 512
                    tev = C.act(lambda e, bk=bk, so=so: e.copy(
                        v_stage[:, so:so + 512], bk.t[:, :]),
                        waits=[(s_pe, tmm), vs_store[r]])
                    bk.free = (s_act, tev)
                    hpe = 512 // 128
                    vs_store[r] = C.stq(m_sv[r],
                                        v_hbm.ap()[b, eb * hpe:(eb + 1) * hpe, :,
                                                   sc * 128:(sc + 1) * 128].rearrange(
                                            "h p e -> p h e"),
                                        v_stage[:, so:so + 512].rearrange(
                                            "p (h e) -> p h e", h=hpe),
                                        waits=[(s_act, tev)])
                    it += 1

            C.barrier()
            qkv_tok = [None, None]
            hb_freed = [None, None]
            a_cons = [None, None, None, None]
            ot_store = [(m, m.n) for m in m_sot]
            lrep_used = [None, None]
            dbg_stok = [None, None, None]
            st_banks = (banks[0], banks[1], banks[6])
            av_banks = (banks[2], banks[3], banks[7])
            l_banks = (banks[4], banks[5])
            iav = [0]
            linv_used = None
            rep_prev = None
            q_tail1 = []
            q_tail2 = []
            tail2_tcp = []
            tcp_pending = None
            repcopy_prev = None
            it_st = 0
            dbg_ix = [0] * H
            for h in range(H):
                bi = h % 2
                C.ld(m_qkv[bi], AB[:, qb_o + bi * S: qb_o + (bi + 1) * S],
                     qT_hbm.ap()[b, h, :, :], waits=[hb_freed[bi]])
                C.ld(m_qkv[bi], AB[:, kb_o + bi * S: kb_o + (bi + 1) * S],
                     kT_hbm.ap()[b, h, :, :])
                qkv_tok[bi] = C.ld(
                    m_qkv[bi],
                    AB[:, vb_o + bi * S: vb_o + (bi + 1) * S],
                    v_hbm.ap()[b, h, :, :])
                qb = AB[:, qb_o + bi * S: qb_o + (bi + 1) * S]
                kb = AB[:, kb_o + bi * S: kb_o + (bi + 1) * S]
                vb = AB[:, vb_o + bi * S: vb_o + (bi + 1) * S]
                last_pe_this_head = None
                for q4 in range(NSB):
                    nci = 4 * q4 + 4
                    avb = av_banks[iav[0] % 3]
                    l_bank = l_banks[iav[0] % 2]
                    iav[0] += 1
                    atok = {}
                    asem = {}

                    def emit_st(ci):
                        nonlocal it_st, last_pe_this_head
                        bk = st_banks[it_st % 3]
                        tmm = C.pe(lambda e, bk=bk, ci=ci, q4=q4, qb=qb, kb=kb: e.matmul(
                            bk.t[:, :], kb[:, ci * 128:(ci + 1) * 128],
                            qb[:, q4 * 512:(q4 + 1) * 512],
                            start=True, stop=True),
                            waits=[qkv_tok[bi], bk.free])
                        it_st += 1
                        last_pe_this_head = tmm
                        ar = ci % 4
                        j = ci - 4 * q4
                        ao = ab_o + ar * 512
                        if j > 0:
                            tz = C.dve(lambda e, ao=ao, j=j: e.memset(
                                AB[:, ao: ao + 128 * j], 0.0),
                                waits=[(s_pe, a_cons[ar])])
                        te = C.act(lambda e, bk=bk, ao=ao, j=max(j, 0): e.activation(
                            AB[:, ao + 128 * j: ao + 512], bk.t[:, 128 * max(j, 0):],
                            AF.Exp, bias=cvec(NEGC), scale=cvec(SCL)),
                            waits=[(s_pe, tmm), (s_pe, a_cons[ar]),
                                   dbg_stok[ar] if debug else None])
                        bk.free = (s_act, te)
                        if j >= 0:
                            tm = C.dve(lambda e, ao=ao, j=j: e.tensor_mul(
                                AB[:, ao + 128 * j: ao + 128 * j + 128],
                                AB[:, ao + 128 * j: ao + 128 * j + 128],
                                masks[:, 384:512]),
                                waits=[(s_act, te), tk_ini])
                            atok[ci] = (tm, te)
                            asem[ci] = None
                            if debug and b == 0:
                                dbg_stok[ar] = C.stq(
                                    m_dbg, dbgA_t.ap()[h, dbg_ix[h], :, :],
                                    AB[:, ao: ao + 512],
                                    waits=[(s_dve, tm)])
                                dbg_ix[h] += 1
                        else:
                            atok[ci] = te
                            asem[ci] = s_act

                    def emit_lav(ci):
                        nonlocal last_pe_this_head
                        ar = ci % 4
                        if asem[ci] is None:
                            wl = [(s_dve, atok[ci][0]), (s_act, atok[ci][1])]
                        else:
                            wl = [(asem[ci], atok[ci])]
                        if ci == 0:
                            wl.append(l_bank.free)
                        C.pe(lambda e, ci=ci, ar=ar, l_bank=l_bank: e.matmul(
                            l_bank.t[0:1, :], ones_cb[:, :],
                            AB[:, ab_o + ar * 512: ab_o + (ar + 1) * 512],
                            start=(ci == 0), stop=(ci == nci - 1),
                            skip_group_check=True),
                            waits=wl, inc=False)
                        wl2 = [qkv_tok[bi]]
                        if ci == 0:
                            wl2.append(avb.free)
                        tav = C.pe(lambda e, ci=ci, ar=ar, avb=avb, vb=vb: e.matmul(
                            avb.t[:, :], vb[:, ci * 128:(ci + 1) * 128],
                            AB[:, ab_o + ar * 512: ab_o + (ar + 1) * 512],
                            start=(ci == 0), stop=(ci == nci - 1),
                            skip_group_check=True),
                            waits=wl2, inc=True)
                        a_cons[ar] = tav
                        last_pe_this_head = tav
                        return tav

                    emit_st(0)
                    emit_st(1)
                    emit_st(2)
                    tav_last = None
                    for ci in range(3, nci):
                        emit_st(ci)
                        tav_last = emit_lav(ci - 3)
                    tav_last = emit_lav(nci - 3)
                    tav_last = emit_lav(nci - 2)
                    tav_last = emit_lav(nci - 1)
                    if q_tail2:
                        q_tail2.pop(0)()
                    if q_tail1:
                        t1, t2 = q_tail1.pop(0)
                        t1()
                        tail2_tcp.append(tcp_pending)
                        q_tail2.append(t2)
                    t_rc = C.dve(lambda e, l_bank=l_bank: e.reciprocal(
                        linv_sb[0:1, :], l_bank.t[0:1, :]),
                                 waits=[(s_pe, tav_last), (s_dve, linv_used),
                                        (s_pe, rep_prev)])
                    t_hi = C.dve(lambda e: e.tensor_copy(linv_hi[0:1, :], linv_sb[0:1, :]),
                                 waits=[(s_dve, t_rc)])
                    t_hf = C.dve(lambda e: e.tensor_copy(linv_hif[0:1, :], linv_hi[0:1, :]),
                                 waits=[(s_dve, t_hi)])
                    t_lo = C.dve(lambda e: e.tensor_sub(linv_lo[0:1, :], linv_sb[0:1, :],
                                                        linv_hif[0:1, :]),
                                 waits=[(s_dve, t_hf)])
                    linv_used = t_lo
                    l_bank.free = (s_dve, t_rc)

                    def tail1(l_bank=l_bank, t_lo=t_lo, q4=q4):
                        nonlocal rep_prev, repcopy_prev, tcp_pending
                        C.pe(lambda e: e.matmul(l_bank.t[:, :], ones_rb[:, :],
                                                linv_hi[0:1, :], start=True, stop=False),
                             waits=[(s_dve, t_lo)], inc=False)
                        t_rep = C.pe(lambda e: e.matmul(l_bank.t[:, :], ones_rb[:, :],
                                                        linv_lo[0:1, :],
                                                        start=False, stop=True))
                        rep_prev = t_rep
                        lr = q4 % 2
                        t_cp = C.dve(lambda e, lr=lr: e.tensor_copy(
                            lrep_sb[:, lr * 512:(lr + 1) * 512], l_bank.t[:, :]),
                            waits=[(s_pe, t_rep), (s_dve, lrep_used[lr])])
                        repcopy_prev = t_cp
                        l_bank.free = (s_dve, t_cp)
                        tcp_pending = t_cp

                    def tail2(avb=avb, q4=q4, h=h, b=b, t1=None):
                        nonlocal tail2_tcp
                        t_cp = tail2_tcp.pop(0)
                        lr = q4 % 2
                        orr = q4 % 2
                        t_ot = C.dve(lambda e, lr=lr, orr=orr: e.tensor_mul(
                            ot_stage[:, orr * 512:(orr + 1) * 512], avb.t[:, :],
                            lrep_sb[:, lr * 512:(lr + 1) * 512]),
                            waits=[(s_dve, t_cp), ot_store[orr]])
                        lrep_used[lr] = t_ot
                        avb.free = (s_dve, t_ot)
                        ot_store[orr] = C.stq(m_sot[orr],
                                              oT_hbm.ap()[b, q4, :, h * 512:(h + 1) * 512],
                                              ot_stage[:, orr * 512:(orr + 1) * 512],
                                              waits=[(s_dve, t_ot)])

                    q_tail1.append((tail1, tail2))
                hb_freed[bi] = (s_pe, last_pe_this_head)
            while q_tail2 or q_tail1:
                if q_tail2:
                    q_tail2.pop(0)()
                if q_tail1:
                    t1, t2 = q_tail1.pop(0)
                    t1()
                    tail2_tcp.append(tcp_pending)
                    q_tail2.append(t2)

            C.barrier()
            ln_state = {}
            gy_tok = []
            sy_tok = []
            otg_tok = [None, None]
            otg_freed = [None, None]
            wop_tok = [None, None]
            wop_freed = [None, None]
            xc_tok = [None, None]
            xc_freed = [None, None]
            itb = 0
            for g in range(NS // 4):
                gb = g % 2
                otg_tok[gb] = C.ld(
                    m_otg[gb],
                    AB[:, otg_o + gb * H * 512: otg_o + (gb + 1) * H * 512],
                    oT_hbm.ap()[b, g, :, :],
                    waits=[otg_freed[gb]])
                otg = AB[:, otg_o + gb * H * 512: otg_o + (gb + 1) * H * 512]
                for eb in range(NEB):
                    wb_i = (g * NEB + eb) % 2
                    woff = wop_o + wb_i * H * 512
                    wop_tok[wb_i] = C.ld(
                        m_wop[wb_i],
                        AB[:, woff:woff + H * 512],
                        wo_bf.ap()[eb * 128:(eb + 1) * 128, :],
                        waits=[tk_wo, wop_freed[wb_i]])
                    for si in range(4):
                        sc = g * 4 + si
                        xr = itb % 2
                        xc_tok[xr] = C.ld(m_xc[xr], xc_sb[:, xr * 512:(xr + 1) * 512],
                                          xap[b, sc * 128:(sc + 1) * 128,
                                              eb * 512:(eb + 1) * 512],
                                          waits=[xc_freed[xr]])
                        bk = banks[itb % 6]
                        for h in range(H):
                            wl = []
                            if h == 0:
                                wl = [otg_tok[gb], wop_tok[wb_i], bk.free]
                            tmm = C.pe(lambda e, bk=bk, h=h, si=si, woff=woff, otg=otg: e.matmul(
                                bk.t[:, :],
                                otg[:, h * 512 + si * 128: h * 512 + si * 128 + 128],
                                AB[:, woff + h * 512: woff + (h + 1) * 512],
                                start=(h == 0), stop=(h == H - 1)),
                                waits=wl, inc=(h == H - 1))
                        wop_freed[wb_i] = (s_pe, tmm)
                        otg_freed[gb] = (s_pe, tmm)
                        p4_last_tok = tmm
                        tev = C.dve(lambda e, bk=bk, si=si, eb=eb, xr=xr: e.tensor_add(
                            FA[:, r1_o + si * D + eb * 512: r1_o + si * D + (eb + 1) * 512],
                            bk.t[:, :], xc_sb[:, xr * 512:(xr + 1) * 512]),
                            waits=[(s_pe, tmm), xc_tok[xr],
                                   (m_sy[si], m_sy[si].n), (m_gy[si], m_gy[si].n)])
                        bk.free = (s_dve, tev)
                        xc_freed[xr] = (s_dve, tev)
                        itb += 1
                rdy = [(s_dve, s_dve.n)]
                for si in range(4):
                    sc = g * 4 + si
                    r_ap = FA[:, r1_o + si * D: r1_o + (si + 1) * D]
                    t_y = layernorm(r_ap, ln_state, rdy=rdy)
                    C.stq(m_sy[si], y_hbm.ap()[b, sc * 128:(sc + 1) * 128, :], r_ap,
                          waits=[(s_dve, t_y)])
                    C.gpd(m_gy[si], yb_hbm.ap()[b, sc * 128:(sc + 1) * 128, :], r_ap,
                          waits=[(s_dve, t_y)])
                gy_tok.append([(m, m.n) for m in m_gy])
                sy_tok.append([(m, m.n) for m in m_sy])

            p4_last = (s_pe, p4_last_tok)
            p4_ev = [xc_freed[0], xc_freed[1]]
            w1_tok = [None, None]
            w1_freed = [p4_last, p4_last]
            w2_tok = [None, None]
            w2_freed = [p4_last, p4_last]
            w3_tok = [None, None]
            w3_freed = [None, None]
            yl_tok = [None, None]
            yl_freed = [p4_ev[0], p4_ev[1]]
            ln2_prev = ln_state
            f_last_prev = p4_last[1]
            h1_last_prev = p4_last[1]
            h1_last_prev = otg_freed[0][1]
            for t5 in range(NT):
                row0 = t5 * STILE
                for dc in range(ND):
                    wl_yt = [(s_pe, h1_last_prev)] + gy_tok[t5]
                    tk_yT = C.ld(m_yt, AB[:, yT_o + dc * STILE: yT_o + (dc + 1) * STILE],
                                 yb_hbm.ap()[b, row0:row0 + STILE,
                                             dc * 128:(dc + 1) * 128],
                                 waits=wl_yt, transpose=True)
                for j in range(NE1):
                    wb_i = j % 2
                    woff = w1c_o + wb_i * ND * 128
                    w1_tok[wb_i] = C.ld(
                        m_w1[wb_i],
                        AB[:, woff:woff + ND * 128],
                        w1_bf.ap()[j * 128:(j + 1) * 128, :],
                        waits=[tk_w1, w1_freed[wb_i]])
                    bk = banks[j % 4]
                    for dc in range(ND):
                        wl = []
                        if dc == 0:
                            wl = [w1_tok[wb_i], tk_yT, bk.free]
                        tmm = C.pe(lambda e, bk=bk, woff=woff, dc=dc: e.matmul(
                            bk.t[:, 0:STILE],
                            AB[:, woff + dc * 128: woff + (dc + 1) * 128],
                            AB[:, yT_o + dc * STILE: yT_o + (dc + 1) * STILE],
                            start=(dc == 0), stop=(dc == ND - 1)),
                            waits=wl, inc=(dc == ND - 1))
                    w1_freed[wb_i] = (s_pe, tmm)
                    tev = C.act(lambda e, bk=bk, j=j: e.activation(
                        AA[:, h1_o + j * STILE: h1_o + (j + 1) * STILE],
                        bk.t[:, 0:STILE], AF.Relu, bias=b1t_sb[:, j:j + 1]),
                        waits=[(s_pe, tmm), tk_ini, (s_pe, f_last_prev),
                               (m_w3[1], m_w3[1].n)])
                    bk.free = (s_act, tev)
                h1_last_prev = w1_freed[0][1] if w1_freed[0] else None
                h1_last_prev = max(h1_last_prev or 0,
                                   w1_freed[1][1] if w1_freed[1] else 0) or None
                tk_h1_done = C.sems["act"].n
                for j2 in range(NE2):
                    wb_i = j2 % 2
                    woff = w2c_o + wb_i * NE1 * 128
                    w2_tok[wb_i] = C.ld(
                        m_w2[wb_i],
                        AB[:, woff:woff + NE1 * 128],
                        w2_bf.ap()[j2 * 128:(j2 + 1) * 128, :],
                        waits=[tk_w2, w2_freed[wb_i]])
                    bk = banks[4 + (j2 % 4)]
                    for ec in range(NE1):
                        wl = []
                        if ec == 0:
                            wl = [w2_tok[wb_i], (s_act, tk_h1_done), bk.free]
                        tmm = C.pe(lambda e, bk=bk, woff=woff, ec=ec: e.matmul(
                            bk.t[:, 0:STILE],
                            AB[:, woff + ec * 128: woff + (ec + 1) * 128],
                            AA[:, h1_o + ec * STILE: h1_o + (ec + 1) * STILE],
                            start=(ec == 0), stop=(ec == NE1 - 1)),
                            waits=wl, inc=(ec == NE1 - 1))
                    w2_freed[wb_i] = (s_pe, tmm)
                    tev = C.act(lambda e, bk=bk, j2=j2: e.activation(
                        AB[:, h2_off(j2): h2_off(j2) + STILE],
                        bk.t[:, 0:STILE], AF.Relu, bias=b2t_sb[:, j2:j2 + 1]),
                        waits=[(s_pe, tmm), tk_ini])
                    bk.free = (s_act, tev)
                tk_h2_done = C.sems["act"].n
                itf = 0
                for eb in range(NEB):
                    wb_i = eb % 2
                    woff = w3slotA if wb_i == 0 else h1_o
                    wl_extra = []
                    if wb_i == 1:
                        wl_extra = [w2_freed[0], w2_freed[1]]
                    w3_tok[wb_i] = C.ld(
                        m_w3[wb_i],
                        AA[:, woff:woff + NE2 * 512],
                        w3_bf.ap()[eb * 128:(eb + 1) * 128, :],
                        waits=[tk_w3, w3_freed[wb_i]] + wl_extra)
                    for si in range(STILE // 128):
                        yr = itf % 2
                        yl_tok[yr] = C.ld(m_yl[yr], xc_sb[:, yr * 512:(yr + 1) * 512],
                                          y_hbm.ap()[b, row0 + si * 128: row0 + (si + 1) * 128,
                                                     eb * 512:(eb + 1) * 512],
                                          waits=[yl_freed[yr]] + sy_tok[t5])
                        bk = banks[itf % 4]
                        for ec in range(NE2):
                            wl = []
                            if ec == 0:
                                wl = [w3_tok[wb_i], (s_act, tk_h2_done), bk.free]
                            C.pe(lambda e, bk=bk, woff=woff, ec=ec, si=si: e.matmul(
                                bk.t[:, :],
                                AB[:, h2_off(ec) + si * 128: h2_off(ec) + si * 128 + 128],
                                AA[:, woff + ec * 512: woff + (ec + 1) * 512],
                                start=(ec == 0), stop=False),
                                waits=wl, inc=False)
                        tmm = C.pe(lambda e, bk=bk, eb=eb: e.matmul(
                            bk.t[:, :], ones_rb[:, :],
                            b3b_sb[0:1, eb * 512:(eb + 1) * 512],
                            start=False, stop=True),
                            waits=[(s_dve, tk_b3b)], inc=True)
                        w3_freed[wb_i] = (s_pe, tmm)
                        f_last = (s_pe, tmm)
                        tev = C.dve(lambda e, bk=bk, si=si, eb=eb, yr=yr: e.tensor_add(
                            FA[:, r1_o + si * D + eb * 512: r1_o + si * D + (eb + 1) * 512],
                            bk.t[:, :], xc_sb[:, yr * 512:(yr + 1) * 512]),
                            waits=[(s_pe, tmm), yl_tok[yr],
                                   (s_act, ln2_prev.get("sqrt")),
                                   (m_sout, m_sout.n),
                                   (m_sy[si], m_sy[si].n), (m_gy[si], m_gy[si].n)])
                        bk.free = (s_dve, tev)
                        yl_freed[yr] = (s_dve, tev)
                        itf += 1
                f_last_prev = f_last[1]
                w1_freed = [f_last, f_last]
                rdy2 = [(s_dve, s_dve.n)]
                for si in range(STILE // 128):
                    r_ap = FA[:, r1_o + si * D: r1_o + (si + 1) * D]
                    t_y = layernorm(r_ap, ln2_prev, rdy=rdy2)
                    C.stq(m_sout, outap[b, row0 + si * 128: row0 + (si + 1) * 128, :],
                          r_ap, waits=[(s_dve, t_y)])
            prev_f_last = f_last_prev
            prev_ln_sqrt = ln_state.get("sqrt")

        C.barrier()
        fin = [(s, s.n) for s in C.all_sems]

        def final_waits(e):
            for s, v in fin:
                if v > 0:
                    e.wait_ge(s.h, v)

        C.ops["sp"].append(final_waits)

        with nc.Block() as blk:
            blk.tensor(lambda e: [f(e) for f in C.ops["pe"]])
            blk.vector(lambda e: [f(e) for f in C.ops["dve"]])
            blk.scalar(lambda e: [f(e) for f in C.ops["act"]])
            blk.gpsimd(lambda e: [f(e) for f in C.ops["gp"]])
            blk.sync(lambda e: [f(e) for f in C.ops["sp"]])
    return nc


def make_consts(cfg):
    S = cfg.S
    inv_freq = 1.0 / (10000.0 ** (np.arange(0, 128, 2, dtype=np.float64) / 128.0))
    t = np.arange(S, dtype=np.float64)
    freqs = np.outer(t, inv_freq)
    cosf = np.cos(freqs).T.astype(np.float32)
    sinf = np.sin(freqs).T.astype(np.float32)
    cos_tab = np.concatenate([cosf, cosf], axis=0)
    sin_tab = np.concatenate([-sinf, sinf], axis=0)
    r = np.arange(128)[:, None]
    u = np.arange(896)[None, :]
    masks = ((u - 384) >= r).astype(np.float32).astype(ml_dtypes.bfloat16)
    return cos_tab, sin_tab, masks


def _panel(wT, nin, nout, ocols):
    return np.ascontiguousarray(
        wT.reshape(nin, 128, nout, ocols).transpose(2, 1, 0, 3)
          .reshape(nout * 128, nin * ocols))


def make_inputs_for_core(cfg, x_shard, Wq, Wk, Wv, Wo, W1, b1, W2, b2, W3, b3):
    cos_tab, sin_tab, masks = make_consts(cfg)
    c = np.ascontiguousarray
    ND, NEB, NE1, NE2, H = cfg.ND, cfg.NEB, cfg.NE1, cfg.NE2, cfg.H
    return {
        "x": c(x_shard.astype(np.float32)),
        "wqT": _panel(Wq.T, ND, H, 128),
        "wkT": _panel(Wk.T, ND, H, 128),
        "wvT": _panel(Wv.T, ND, NEB, 512),
        "woT": _panel(Wo.T, H, NEB, 512),
        "w1T": _panel(W1.T, ND, NE1, 128),
        "w2T": _panel(W2.T, NE1, NE2, 128),
        "w3T": _panel(W3.T, NE2, NEB, 512),
        "b1t": c(b1.reshape(cfg.NE1, 128).T),
        "b2t": c(b2.reshape(cfg.NE2, 128).T),
        "b3row": c(b3.reshape(1, cfg.D)),
        "cos_t": cos_tab, "sin_t": sin_tab, "masks": masks,
    }


def kernel(L0_attn_x, L0_attn_Wq, L0_attn_Wk, L0_attn_Wv, L0_attn_Wo,
           L0_ffn_W1, L0_ffn_b1, L0_ffn_W2, L0_ffn_b2, L0_ffn_W3, L0_ffn_b3):
    cfg = FULL
    n_cores = 8
    x = np.asarray(L0_attn_x, dtype=np.float32)
    args = [np.asarray(a, dtype=np.float32) for a in
            (L0_attn_Wq, L0_attn_Wk, L0_attn_Wv, L0_attn_Wo,
             L0_ffn_W1, L0_ffn_b1, L0_ffn_W2, L0_ffn_b2, L0_ffn_W3, L0_ffn_b3)]
    nc = build_program(cfg)
    in_maps = []
    base = None
    for i in range(n_cores):
        shard = x[i * cfg.B:(i + 1) * cfg.B]
        if base is None:
            base = make_inputs_for_core(cfg, shard, *args)
            m = base
        else:
            m = dict(base)
            m["x"] = np.ascontiguousarray(shard)
        in_maps.append(m)
    res = run_bass_kernel_spmd(nc, in_maps, core_ids=list(range(n_cores)))
    return np.concatenate([r["out"] for r in res.results], axis=0)
